# version 03
# baseline (speedup 1.0000x reference)
from contextlib import ExitStack

import numpy as np
import ml_dtypes
import concourse.bass as bass
import concourse.mybir as mybir
from concourse.bass_utils import run_bass_kernel_spmd

F32 = mybir.dt.float32
BF16 = mybir.dt.bfloat16
AF = mybir.ActivationFunctionType
ALU = mybir.AluOpType
AX = mybir.AxisListType

B, T, C = 32, 2048, 1024
H, DH = 16, 64
NHP = 8
FF = 4096
NE = 8
NCORE = 8
SEQ_PER_CORE = B // NCORE
LN_EPS = 1e-5
ROPE_THETA = 10000.0
NCH = T // 512
NTT = T // 128
BIG = 1.0e30
SWAP_MASK = [i ^ 1 for i in range(32)]


class Stream:
    def __init__(self, ctx, nc, name, owner, chain_eng=None):
        self.h = ctx.enter_context(nc.semaphore(name))
        self.val = 0
        self.owner = owner
        self.chain_eng = chain_eng

    def sig(self, instr, n=1):
        instr.then_inc(self.h, n)
        self.val += n
        if self.chain_eng is not None:
            self.chain_eng.wait_ge(self.h, self.val)
        return (self, self.val)

    def tok(self):
        return (self, self.val)


class Builder:
    def __init__(self, nc, ctx, n_seq, debug_outs):
        self.nc = nc
        self.ctx = ctx
        self.n_seq = n_seq
        self.debug_outs = set(debug_outs)
        self.dbg_names = []
        self.sT = Stream(ctx, nc, "sT", "tensor")
        self.sV = Stream(ctx, nc, "sV", "vector", chain_eng=nc.vector)
        self.sA = Stream(ctx, nc, "sA", "scalar", chain_eng=nc.scalar)
        self.sP = Stream(ctx, nc, "sP", "gpsimd", chain_eng=nc.gpsimd)
        self.dma = {name: Stream(ctx, nc, "d_" + name, None) for name in (
            "const", "nat", "xrow0", "xrow1", "wq0", "wq1",
            "w1s0", "w1s1", "w2s0", "w2s1", "spill", "reload", "out",
            "cast", "cast2", "tr0", "tr1", "st0", "st1")}
        self.dma_bg = {"cast", "cast2", "tr0", "tr1", "st0", "st1"}
        self.cast_q = []
        self._guar = {}

    def pump(self, n):
        while n > 0 and self.cast_q:
            self.cast_q.pop(0)()
            n -= 1

    def wait(self, eng_name, *tokens):
        e = getattr(self.nc, eng_name)
        for t in tokens:
            if t is None:
                continue
            st, val = t
            if val <= 0:
                continue
            if st.owner == eng_name:
                continue
            key = (eng_name, id(st))
            if self._guar.get(key, -1) >= val:
                continue
            e.wait_ge(st.h, val)
            self._guar[key] = val

    def all_toks(self, bg=False):
        return ([s.tok() for s in (self.sT, self.sV, self.sA, self.sP)]
                + [s.tok() for n, s in self.dma.items()
                   if bg or n not in self.dma_bg])

    def barrier(self, bg=False):
        toks = self.all_toks(bg=bg)
        for en in ("tensor", "vector", "scalar", "gpsimd", "sync"):
            self.wait(en, *toks)

    def debug_dump(self, name, ap, shape, dtype=F32):
        if name not in self.debug_outs:
            return
        t = self.nc.dram_tensor("dbg_" + name, list(shape), dtype,
                                kind="ExternalOutput").ap()
        self.dbg_names.append("dbg_" + name)
        self.wait("gpsimd", *self.all_toks(bg=True))
        tk = self.dma["out"].sig(self.nc.gpsimd.dma_start(t[:], ap), 16)
        for en in ("tensor", "vector", "scalar", "gpsimd", "sync"):
            self.wait(en, tk)


def host_constants():
    i = np.arange(0, DH, 2, dtype=np.float32) / DH
    inv = (1.0 / ROPE_THETA) ** i
    t = np.arange(T, dtype=np.float32)
    freqs = np.outer(t, inv)
    cos = np.cos(freqs)
    sin = np.sin(freqs)
    rows = np.arange(128)
    pair = (rows % 64) // 2
    sign = np.where(rows % 2 == 0, -1.0, 1.0).astype(np.float32)
    cos_t = np.ascontiguousarray(cos[:, pair].T.astype(np.float32))
    sin_t = np.ascontiguousarray(
        (sin[:, pair].T * sign[:, None]).astype(np.float32))
    p = np.arange(128)[:, None]
    c = np.arange(512)[None, :]
    mask01 = np.ascontiguousarray(
        np.stack([(128 * jj + p <= c) for jj in range(4)], axis=1)
        .astype(ml_dtypes.bfloat16))
    ident32 = np.eye(128, dtype=np.float32)
    ident_bf = np.eye(128, dtype=ml_dtypes.bfloat16)
    ones_col = np.ones((128, 1), dtype=np.float32)
    ones_row = np.ones((1, 128), dtype=np.float32)
    sel = np.zeros((8, 1024), dtype=np.float32)
    for e in range(8):
        sel[e, e * 128:(e + 1) * 128] = 1.0
    return dict(cos_t=cos_t, sin_t=sin_t, mask01=mask01, ident32=ident32,
                ident_bf=ident_bf, ones_col=ones_col, ones_row=ones_row,
                sel=sel)


def build_program(n_seq=SEQ_PER_CORE, debug_outs=()):
    nc = bass.Bass("TRN2", target_bir_lowering=False, debug=False)

    def _uniq(fn):
        cnt = [0]

        def wrap(name, *a, **k):
            cnt[0] += 1
            return fn(f"{name}__{cnt[0]}", *a, **k)
        return wrap

    nc.sbuf_tensor = _uniq(nc.sbuf_tensor)
    nc.psum_tensor = _uniq(nc.psum_tensor)
    ctx = ExitStack()
    b = Builder(nc, ctx, n_seq, debug_outs)

    d = {}
    for name, shape, dt in [
        ("x", [n_seq, T, C], F32), ("w_qkv", [C, 3 * C], F32),
        ("w_out", [C, C], F32), ("Wr", [NE, C], F32), ("br", [1, NE], F32),
        ("W1", [NE, FF, C], F32), ("b1", [NE, FF], F32),
        ("W2", [NE, C, FF], F32), ("b2", [NE, C], F32),
        ("gamma", [8, 128], F32), ("beta", [8, 128], F32),
        ("cos_t", [128, T], F32), ("sin_t", [128, T], F32),
        ("mask01", [128, 4, 512], BF16), ("ident32", [128, 128], F32),
        ("ident_bf", [128, 128], BF16), ("ones_col", [128, 1], F32),
        ("ones_row", [1, 128], F32), ("sel", [8, 1024], F32),
    ]:
        d[name] = nc.dram_tensor(name, shape, dt, kind="ExternalInput").ap()
    y_out = nc.dram_tensor("y", [n_seq, T, C], F32, kind="ExternalOutput").ap()
    w1t = nc.dram_tensor("w1t_s", [NE, 8, 128, FF], BF16, kind="Internal").ap()
    w2t = nc.dram_tensor("w2t_s", [NE, 32, 128, C], BF16, kind="Internal").ap()
    wbf1 = nc.dram_tensor("wbf1_s", [NE, FF, C], BF16, kind="Internal").ap()
    wbf2 = nc.dram_tensor("wbf2_s", [NE, C, FF], BF16, kind="Internal").ap()
    x1sp = nc.dram_tensor("x1_sp", [n_seq, NCH, 128, 8, 512], F32,
                          kind="Internal").ap()

    with ctx:
        g = {}
        E_ = ctx.enter_context
        g["ident32"] = E_(nc.sbuf_tensor("ident32", [128, 128], F32))
        g["ident_bf"] = E_(nc.sbuf_tensor("ident_bf", [128, 128], BF16))
        g["ones_col"] = E_(nc.sbuf_tensor("ones_col", [128, 1], F32))
        g["ones_colbf"] = E_(nc.sbuf_tensor("ones_colbf", [128, 1], BF16))
        g["ones_row"] = E_(nc.sbuf_tensor("ones_row", [1, 128], F32))
        g["sel"] = E_(nc.sbuf_tensor("sel", [8, 1024], F32))
        g["woutbf"] = E_(nc.sbuf_tensor("woutbf", [128, 8, C], BF16))
        g["b1sb"] = E_(nc.sbuf_tensor("b1sb", [128, 32, 8], F32))
        g["b2sb"] = E_(nc.sbuf_tensor("b2sb", [128, 8, 8], F32))
        g["wrtbf"] = E_(nc.sbuf_tensor("wrtbf", [128, 8, 8], BF16))
        g["brb"] = E_(nc.sbuf_tensor("brb", [128, 8], F32))
        g["gsb"] = E_(nc.sbuf_tensor("gsb", [128, 8], F32))
        g["bsb"] = E_(nc.sbuf_tensor("bsb", [128, 8], F32))
        g["x1bf"] = E_(nc.sbuf_tensor("x1bf", [128, 8, T], BF16))
        g["bounce"] = [E_(nc.sbuf_tensor(f"bounce{i}", [128, 1024], BF16))
                       for i in range(2)]

        phase0(b, d, g, w1t, w2t, wbf1, wbf2)

        for s in range(n_seq):
            with ExitStack() as seq_ctx:
                Es = seq_ctx.enter_context
                sq = {}
                sq["cos_t"] = Es(nc.sbuf_tensor("cos_t", [128, T], F32))
                sq["sin_t"] = Es(nc.sbuf_tensor("sin_t", [128, T], F32))
                sq["mask01"] = Es(nc.sbuf_tensor("mask01", [128, 4, 512], BF16))
                sq["v_tok"] = [Es(nc.sbuf_tensor(f"v_tok{i}",
                                                 [128, 16, 130], BF16))
                               for i in range(2)]
                sq["o_full"] = Es(nc.sbuf_tensor("o_full", [128, 8, T], BF16))
                dc = b.dma["const"]
                dc.sig(nc.gpsimd.dma_start(sq["cos_t"][:], d["cos_t"][:]), 16)
                dc.sig(nc.gpsimd.dma_start(sq["sin_t"][:], d["sin_t"][:]), 16)
                dc.sig(nc.gpsimd.dma_start(sq["mask01"][:], d["mask01"][:]), 16)
                for vt in sq["v_tok"]:
                    b.sP.sig(nc.gpsimd.memset(vt[:, :, 64:65], 1.0))
                    o2 = b.sP.sig(nc.gpsimd.memset(vt[:, :, 129:130], 1.0))
                sq["const_dma"] = dc.tok()
                sq["const_mem"] = o2

                attn_phase(b, d, g, sq, s)
                ln1_phase(b, d, g, sq, s, x1sp)
            ffn_phase(b, d, g, s, w1t, w2t, x1sp, y_out)
            b.barrier()

        b.wait("gpsimd", *[s.tok() for s in b.dma.values()])
        b.wait("sync", *[s.tok() for s in b.dma.values()])
        b.wait("tensor", *[s.tok() for s in b.dma.values()])
    return nc, b


def phase0(b, d, g, w1t, w2t, wbf1, wbf2):
    nc = b.nc
    for dst, src in [
        (g["ident32"], d["ident32"]), (g["ident_bf"], d["ident_bf"]),
        (g["ones_col"], d["ones_col"]), (g["ones_row"], d["ones_row"]),
        (g["sel"], d["sel"]),
    ]:
        b.dma["const"].sig(nc.gpsimd.dma_start(dst[:], src[:]), 16)
    b.dma["const"].sig(nc.gpsimd.dma_start(
        g["woutbf"][:], d["w_out"].rearrange("(ct p) c -> p ct c", p=128)), 16)
    b.sP.sig(nc.gpsimd.memset(g["ones_colbf"][:], 1.0))
    tok_small = b.dma["const"].tok()

    with ExitStack() as ph:
        Ep = ph.enter_context
        nat8 = Ep(nc.sbuf_tensor("nat8", [8, 4096], F32))
        pt8 = Ep(nc.psum_tensor("pt8", [128, 512], F32))

        def small_transposes(src_ap, ncols, emit_evac):
            b.wait("gpsimd", b.sT.tok())
            ld = b.dma["nat"].sig(nc.gpsimd.dma_start(nat8[0:8, 0:ncols], src_ap), 16)
            for i in range(ncols // 128):
                b.wait("tensor", ld, tok_small, b.sV.tok())
                b.sT.sig(nc.tensor.transpose(
                    pt8[:, 0:8], nat8[0:8, i * 128:(i + 1) * 128],
                    g["ident32"][0:8, 0:8]))
                b.wait("vector", b.sT.tok())
                emit_evac(i)

        small_transposes(d["b1"][:, :], 4096,
                         lambda i: b.sV.sig(nc.vector.tensor_copy(
                             g["b1sb"][:, i, :], pt8[:, 0:8])))
        small_transposes(d["b2"][:, :], 1024,
                         lambda i: b.sV.sig(nc.vector.tensor_copy(
                             g["b2sb"][:, i, :], pt8[:, 0:8])))
        small_transposes(d["Wr"][:, :], 1024,
                         lambda i: b.sV.sig(nc.vector.tensor_copy(
                             g["wrtbf"][:, i, :], pt8[:, 0:8])))
        small_transposes(d["gamma"][:, :], 128,
                         lambda i: b.sV.sig(nc.vector.tensor_copy(
                             g["gsb"][:, :], pt8[:, 0:8])))
        small_transposes(d["beta"][:, :], 128,
                         lambda i: b.sV.sig(nc.vector.tensor_copy(
                             g["bsb"][:, :], pt8[:, 0:8])))
        b.wait("gpsimd", b.sT.tok())
        ld = b.dma["nat"].sig(nc.gpsimd.dma_start(nat8[0:1, 0:8], d["br"][:, :]), 16)
        b.wait("tensor", ld, tok_small, b.sV.tok())
        b.sT.sig(nc.tensor.matmul(pt8[:, 0:8], g["ones_row"][:],
                                  nat8[0:1, 0:8], start=True, stop=True))
        b.wait("vector", b.sT.tok())
        b.sV.sig(nc.vector.tensor_copy(g["brb"][:, :], pt8[:, 0:8]))
        b.barrier()

    for e in range(NE):
        for fq in range(4):
            def mk1(e=e, fq=fq):
                def f():
                    b.dma["cast"].sig(nc.gpsimd.dma_start(
                        wbf1[e][fq * 1024:(fq + 1) * 1024, :],
                        d["W1"][e][fq * 1024:(fq + 1) * 1024, :]), 16)
                return f
            b.cast_q.append(mk1())
    for e in range(NE):
        for q4 in range(4):
            def mk2(e=e, q4=q4):
                def f():
                    b.dma["cast2"].sig(nc.gpsimd.dma_start(
                        wbf2[e][q4 * 256:(q4 + 1) * 256, :],
                        d["W2"][e][q4 * 256:(q4 + 1) * 256, :]), 16)
                return f
            b.cast_q.append(mk2())
    cast1 = (b.dma["cast"], b.dma["cast"].val + 32 * 16)
    cast2 = (b.dma["cast2"], b.dma["cast2"].val + 32 * 16)
    bounce = g["bounce"]
    st_tok = [None, None]
    i = 0
    for e in range(NE):
        for dt in range(8):
            for fq in range(4):
                r = i % 2
                i += 1
                b.wait("sync", st_tok[r], cast1)
                ttok = b.dma[f"tr{r}"].sig(nc.sync.dma_start(
                    bounce[r][:],
                    wbf1[e][fq * 1024:(fq + 1) * 1024,
                            dt * 128:(dt + 1) * 128], transpose=True), 16)
                b.wait("sync", ttok)
                st_tok[r] = b.dma[f"st{r}"].sig(nc.sync.dma_start(
                    w1t[e, dt, :, fq * 1024:(fq + 1) * 1024], bounce[r][:]),
                    16)
    for e in range(NE):
        for ft in range(32):
            r = i % 2
            i += 1
            b.wait("sync", st_tok[r], cast2)
            ttok = b.dma[f"tr{r}"].sig(nc.sync.dma_start(
                bounce[r][:], wbf2[e][:, ft * 128:(ft + 1) * 128],
                transpose=True), 16)
            b.wait("sync", ttok)
            st_tok[r] = b.dma[f"st{r}"].sig(nc.sync.dma_start(
                w2t[e, ft], bounce[r][:]), 16)


def attn_phase(b, d, g, sq, s):
    nc = b.nc
    with ExitStack() as ph:
        Ep = ph.enter_context
        xbf = Ep(nc.sbuf_tensor("xbf", [128, 8, T], BF16))

        with ExitStack() as p0:
            E0 = p0.enter_context
            xrow = [E0(nc.sbuf_tensor(f"xrow{i}", [128, C], F32))
                    for i in range(2)]
            ptx = [E0(nc.psum_tensor(f"ptx{i}", [128, 512], F32))
                   for i in range(2)]
            row_used = [None, None]
            pt_evac = [None, None]
            for tt in range(NTT):
                sl = tt % 2
                b.wait("gpsimd", row_used[sl])
                ld = b.dma[f"xrow{sl}"].sig(nc.gpsimd.dma_start(
                    xrow[sl][:], d["x"][s, tt * 128:(tt + 1) * 128, :]), 16)
                b.pump(4)
                for dt in range(8):
                    pp = dt % 2
                    b.wait("tensor", ld, pt_evac[pp])
                    tk = b.sT.sig(nc.tensor.transpose(
                        ptx[pp][:, 0:128],
                        xrow[sl][:, dt * 128:(dt + 1) * 128], g["ident32"][:]))
                    dst = xbf[:, dt, tt * 128:(tt + 1) * 128]
                    if dt % 2 == 0:
                        b.wait("vector", tk)
                        pt_evac[pp] = b.sV.sig(
                            nc.vector.tensor_copy(dst, ptx[pp][:, 0:128]))
                    else:
                        b.wait("scalar", tk)
                        pt_evac[pp] = b.sA.sig(
                            nc.scalar.copy(dst, ptx[pp][:, 0:128]))
                row_used[sl] = b.sT.tok()
        b.barrier()
        xbf_tok = (b.sV.tok(), b.sA.tok())
        b.debug_dump(f"xbf{s}", xbf[:], [128, 8, T], BF16)

        with ExitStack() as p1:
            E1 = p1.enter_context
            wq = [E1(nc.sbuf_tensor(f"wq{i}", [128, 8, 384], BF16))
                  for i in range(2)]
            qbf = [E1(nc.sbuf_tensor(f"qbf{i}", [128, T], BF16))
                   for i in range(2)]
            kbf = [E1(nc.sbuf_tensor(f"kbf{i}", [128, T], BF16))
                   for i in range(2)]
            sqb = E1(nc.sbuf_tensor("sqb", [128, T], BF16))
            tq = [E1(nc.sbuf_tensor(f"tq{i}", [128, 512], F32))
                  for i in range(2)]
            tsh = E1(nc.sbuf_tensor("tsh", [128, 512], F32))
            tcs = E1(nc.sbuf_tensor("tcs", [128, 512], F32))
            tsn = E1(nc.sbuf_tensor("tsn", [128, 512], F32))
            e_pair = [E1(nc.sbuf_tensor(f"ep{i}", [128, 2, 512], BF16))
                      for i in range(2)]
            me_ring = [E1(nc.sbuf_tensor(f"mer{i}", [128, 512], BF16))
                       for i in range(2)]
            maxs = E1(nc.sbuf_tensor("maxs", [1, 24], F32))
            rcol = E1(nc.sbuf_tensor("rcol", [128, 4], F32))
            recip_sb = E1(nc.sbuf_tensor("recip_sb", [1, 512], F32))
            rb_sb = E1(nc.sbuf_tensor("rb_sb", [64, 512], F32))
            pqk = [E1(nc.psum_tensor(f"pqk{i}", [128, 512], F32))
                   for i in range(2)]
            psc = [E1(nc.psum_tensor(f"psc{i}", [128, 2, 512], F32))
                   for i in range(2)]
            paux = E1(nc.psum_tensor("paux", [128, 2, 512], F32))

            st = {
                "wq_used": [None, None],
                "ring_cons": [None, None],
                "pqk_free": [None, None],
                "psc_free": [None, None],
                "ep_free": [None, None],
                "me_free": [None, None],
                "paux_free": [None, None],
                "grp": 0,
                "rb_free": None,
                "o_cons": None,
            }

            def prep_units(i):
                ri = i % 2
                sl = i % 2
                units = []

                def u_load():
                    b.wait("gpsimd", st["wq_used"][sl])
                    for ii, off in enumerate((0, C, 2 * C)):
                        b.dma[f"wq{sl}"].sig(nc.gpsimd.dma_start(
                            wq[sl][:, :, ii * 128:(ii + 1) * 128],
                            d["w_qkv"].rearrange("(dt p) c3 -> p dt c3",
                                                 p=128)[
                                :, :, off + i * 128: off + (i + 1) * 128]),
                            16)
                    b.pump(8)
                    st["wtok"] = b.dma[f"wq{sl}"].tok()
                units.append(u_load)

                def mk_qk(m, ch):
                    def u():
                        r = ch % 2
                        dstbuf = qbf[ri] if m == 0 else kbf[ri]
                        b.wait("tensor", st["wtok"], st["pqk_free"][r],
                               st["ring_cons"][ri], *xbf_tok)
                        mm = None
                        for dt in range(8):
                            mm = nc.tensor.matmul(
                                pqk[r][:],
                                wq[sl][:, dt, m * 128:(m + 1) * 128],
                                xbf[:, dt, ch * 512:(ch + 1) * 512],
                                start=(dt == 0), stop=(dt == 7))
                        mmk = b.sT.sig(mm)
                        b.wait("scalar", mmk, st.get("tq_free" + str(r)),
                               st.get("tq_freeV" + str(r)))
                        cpk = b.sA.sig(nc.scalar.copy(tq[r][:], pqk[r][:]))
                        st["pqk_free"][r] = cpk
                        b.wait("vector", cpk, st.get("tsh_free"))
                        shk = b.sV.sig(nc.vector.stream_shuffle(
                            tsh[:], tq[r][:], SWAP_MASK))
                        st["tq_freeV" + str(r)] = shk
                        b.wait("gpsimd", cpk, st.get("tcs_free"),
                               sq["const_dma"], st["ring_cons"][ri])
                        gk = b.sP.sig(nc.gpsimd.tensor_mul(
                            tcs[:], tq[r][:],
                            sq["cos_t"][:, ch * 512:(ch + 1) * 512]))
                        st["tq_free" + str(r)] = gk
                        b.wait("vector", gk, sq["const_dma"],
                               st.get("tsn_free"))
                        vk = b.sV.sig(nc.vector.tensor_mul(
                            tsn[:], tsh[:],
                            sq["sin_t"][:, ch * 512:(ch + 1) * 512]))
                        st["tsh_free"] = vk
                        b.wait("gpsimd", vk)
                        ak = b.sP.sig(nc.gpsimd.tensor_add(
                            dstbuf[:, ch * 512:(ch + 1) * 512], tcs[:],
                            tsn[:]))
                        st["tcs_free"] = ak
                        st["tsn_free"] = ak
                        st["rope_last"] = ak
                        st["ropeL" + str(ri)] = ak
                    return u
                for m in range(2):
                    for ch in range(NCH):
                        units.append(mk_qk(m, ch))

                def mk_v(tt4):
                    def u():
                        vtk = sq["v_tok"][ri]
                        for tt in range(tt4 * 4, tt4 * 4 + 4):
                            r = tt % 2
                            b.wait("tensor", st["wtok"], st["pqk_free"][r],
                                   st["ring_cons"][ri], *xbf_tok)
                            mm = None
                            for dt in range(8):
                                mm = nc.tensor.matmul(
                                    pqk[r][:, 0:128],
                                    xbf[:, dt, tt * 128:(tt + 1) * 128],
                                    wq[sl][:, dt, 256:384],
                                    start=(dt == 0), stop=(dt == 7))
                            mmk = b.sT.sig(mm)
                            dst = vtk[:, tt, 0:130].rearrange(
                                "p (h c) -> p h c", h=2)[:, :, 0:64]
                            srcv = pqk[r][:, 0:128].rearrange(
                                "p (h c) -> p h c", h=2)
                            if tt % 2 == 0:
                                b.wait("vector", mmk)
                                st["pqk_free"][r] = b.sV.sig(
                                    nc.vector.tensor_copy(dst, srcv))
                            else:
                                b.wait("scalar", mmk)
                                st["pqk_free"][r] = b.sA.sig(
                                    nc.scalar.copy(dst, srcv))
                    return u
                def mk_v_wrap(tt4):
                    base = mk_v(tt4)

                    def u():
                        base()
                        if tt4 == 3:
                            st["vtokV" + str(ri)] = b.sV.tok()
                            st["vtokA" + str(ri)] = b.sA.tok()
                    return u
                for tt4 in range(4):
                    units.append(mk_v_wrap(tt4))

                def mk_sq(m):
                    def u():
                        src = qbf[ri] if m == 0 else kbf[ri]
                        b.wait("vector", st["rope_last"],
                               st.get("sqb_free"))
                        st["sq_tok"] = b.sV.sig(nc.vector.tensor_mul(
                            sqb[:], src[:], src[:]))
                    return u

                def mk_norm(m, h):
                    def u():
                        mh = 2 * m + h
                        for ch in range(NCH):
                            b.wait("tensor", st["sq_tok"],
                                   st.get("prr_free"), st["pqk_free"][0])
                            tk = b.sT.sig(nc.tensor.matmul(
                                pqk[0][0:1, :],
                                g["ones_colbf"][64 * h:64 * h + 64, :],
                                sqb[64 * h:64 * h + 64,
                                    ch * 512:(ch + 1) * 512],
                                start=True, stop=True))
                            b.wait("vector", tk)
                            st["prr_free"] = b.sV.sig(nc.vector.reduce_max(
                                maxs[0:1, mh * 4 + ch:mh * 4 + ch + 1],
                                pqk[0][0:1, :], axis=AX.X))
                        b.wait("vector", st["prr_free"])
                        st["mh_tok"] = b.sV.sig(nc.vector.reduce_max(
                            maxs[0:1, 16 + mh:17 + mh],
                            maxs[0:1, mh * 4:mh * 4 + 4], axis=AX.X))
                        if h == 1:
                            st["sqb_free"] = b.sT.tok()
                    return u
                units.append(mk_sq(0))
                units.append(mk_norm(0, 0))
                units.append(mk_norm(0, 1))
                units.append(mk_sq(1))
                units.append(mk_norm(1, 0))
                units.append(mk_norm(1, 1))

                def u_rcol():
                    b.wait("vector", st["mh_tok"])
                    b.sV.sig(nc.vector.tensor_add(
                        maxs[0:1, 20:22], maxs[0:1, 16:18], maxs[0:1, 18:20]))
                    vtk = b.sV.sig(nc.vector.tensor_scalar(
                        maxs[0:1, 22:24], maxs[0:1, 20:22], -1.0 / 16.0, None,
                        op0=ALU.mult))
                    for h in range(2):
                        b.wait("tensor", vtk, st.get("prr_free"),
                               st["ring_cons"][ri])
                        tk = b.sT.sig(nc.tensor.matmul(
                            pqk[0][:, 0:1], g["ones_row"][:],
                            maxs[0:1, 22 + h:23 + h], start=True, stop=True))
                        b.wait("scalar", tk)
                        cpk = b.sA.sig(nc.scalar.copy(
                            rcol[:, 2 * ri + h:2 * ri + h + 1],
                            pqk[0][:, 0:1]))
                        st["prr_free"] = cpk
                    st["pqk_free"][0] = st["prr_free"]
                units.append(u_rcol)
                return units

            def score_units(i):
                ri = i % 2
                vtk = sq["v_tok"][ri]
                units = []

                def mk_group(h, gch):
                    def u():
                        njt = 4 * gch + 4
                        pg = st["grp"] % 2
                        st["grp"] += 1
                        av_tok = None
                        ep_av = st["ep_free"]
                        for jp in range(njt // 2):
                            es = jp % 2
                            b.wait("tensor", st["psc_free"][es],
                                   st.get("ropeL" + str(ri)),
                                   st.get("vtokV" + str(ri)),
                                   st.get("vtokA" + str(ri)))
                            nc.tensor.matmul(
                                psc[es][:, 0, :],
                                kbf[ri][64 * h:64 * h + 64,
                                        256 * jp:256 * jp + 128],
                                qbf[ri][64 * h:64 * h + 64,
                                        gch * 512:(gch + 1) * 512],
                                start=True, stop=True)
                            smk = b.sT.sig(nc.tensor.matmul(
                                psc[es][:, 1, :],
                                kbf[ri][64 * h:64 * h + 64,
                                        256 * jp + 128:256 * jp + 256],
                                qbf[ri][64 * h:64 * h + 64,
                                        gch * 512:(gch + 1) * 512],
                                start=True, stop=True))
                            b.wait("scalar", smk, ep_av[es])
                            etk = b.sA.sig(nc.scalar.activation(
                                e_pair[es][:, :, :], psc[es][:, :, :],
                                AF.Exp, bias=rcol[:, 2 * ri + h:2 * ri + h + 1],
                                scale=0.125))
                            st["psc_free"][es] = etk
                            for jj in range(2):
                                j = 2 * jp + jj
                                src = e_pair[es][:, jj, :]
                                if j >= njt - 4:
                                    mr = j % 2
                                    b.wait("gpsimd", etk, st["me_free"][mr],
                                           sq["const_dma"])
                                    mtk = b.sP.sig(nc.gpsimd.tensor_mul(
                                        me_ring[mr][:], src,
                                        sq["mask01"][:, j - (njt - 4), :]))
                                    src = me_ring[mr][:]
                                    b.wait("tensor", mtk,
                                           st["paux_free"][pg] if j == 0
                                           else None)
                                    av_tok = b.sT.sig(nc.tensor.matmul(
                                        paux[0:65, pg, :],
                                        vtk[:, j, 65 * h:65 * h + 65],
                                        src, start=(j == 0),
                                        stop=(j == njt - 1)))
                                    st["me_free"][mr] = av_tok
                                else:
                                    b.wait("tensor", etk,
                                           st["paux_free"][pg] if j == 0
                                           else None)
                                    av_tok = b.sT.sig(nc.tensor.matmul(
                                        paux[0:65, pg, :],
                                        vtk[:, j, 65 * h:65 * h + 65],
                                        src, start=(j == 0),
                                        stop=(j == njt - 1)))
                            ep_av[es] = av_tok
                        b.wait("vector", av_tok, st.get("recip_free"))
                        rc = b.sV.sig(nc.vector.reciprocal(
                            recip_sb[0:1, :], paux[64:65, pg, :]))
                        b.wait("tensor", rc, st["paux_free"][1 - pg])
                        btk = b.sT.sig(nc.tensor.matmul(
                            paux[0:64, 1 - pg, :], g["ones_row"][0:1, 0:64],
                            recip_sb[0:1, :], start=True, stop=True))
                        st["recip_free"] = btk
                        b.wait("scalar", btk, st["rb_free"])
                        cpk = b.sA.sig(nc.scalar.copy(
                            rb_sb[:], paux[0:64, 1 - pg, :]))
                        st["paux_free"][1 - pg] = cpk
                        b.wait("vector", cpk)
                        ok = b.sV.sig(nc.vector.tensor_mul(
                            sq["o_full"][64 * h:64 * h + 64, i,
                                         gch * 512:(gch + 1) * 512],
                            paux[0:64, pg, :], rb_sb[:]))
                        st["paux_free"][pg] = ok
                        st["rb_free"] = ok
                        st["o_cons"] = ok
                    return u
                for h in range(2):
                    for gch in range(NCH):
                        units.append(mk_group(h, gch))
                return units

            for i in range(NHP + 1):
                pu = prep_units(i) if i < NHP else []
                su = score_units(i - 1) if i >= 1 else []
                out = []
                pi = si = 0
                while pi < len(pu) or si < len(su):
                    for _ in range(3):
                        if pi < len(pu):
                            out.append(pu[pi]); pi += 1
                    if si < len(su):
                        out.append(su[si]); si += 1
                for u in out:
                    u()
                if i >= 1:
                    st["ring_cons"][(i - 1) % 2] = b.sT.tok()
                    st["wq_used"][(i - 1) % 2] = b.sT.tok()
        b.debug_dump(f"ofull{s}", sq["o_full"][:], [128, 8, T], BF16)


def _ln_stats(b, g, s_sb, sqt, pst, pst2, add_toks, pbank):
    nc = b.nc
    mm = None
    for c2t in range(8):
        b.wait("tensor", add_toks[c2t], pbank.get("pst"))
        mm = nc.tensor.matmul(pst[0:1, :], g["ones_col"][:], s_sb[:, c2t, :],
                              start=(c2t == 0), stop=(c2t == 7))
    t1 = b.sT.sig(mm)
    sq_toks = []
    mm = None
    for c2t in range(8):
        r = c2t % 2
        b.wait("vector", add_toks[c2t],
               sq_toks[c2t - 2] if c2t >= 2 else None)
        vt = b.sV.sig(nc.vector.tensor_mul(
            sqt[r][:], s_sb[:, c2t, :], s_sb[:, c2t, :]))
        b.wait("tensor", vt, pbank.get("pst2") if c2t == 0 else None)
        mm = nc.tensor.matmul(pst2[0:1, :], g["ones_col"][:], sqt[r][:],
                              start=(c2t == 0), stop=(c2t == 7))
        sq_toks.append(b.sT.sig(mm))
    return (t1, sq_toks[-1])


def _ln_rows_and_bcast(b, g, rows, pst, pst2, mub, rsb, stat_done, pbank,
                       n=1024.0):
    nc = b.nc
    t1, t2 = stat_done
    b.wait("scalar", t1, t2)
    a1 = b.sA.sig(nc.scalar.mul(rows[0:1, 0:512], pst[0:1, :], 1.0 / n))
    a2 = b.sA.sig(nc.scalar.mul(rows[0:1, 512:1024], pst2[0:1, :], 1.0 / n))
    b.wait("vector", a2)
    b.sV.sig(nc.vector.tensor_mul(rows[0:1, 1024:1536], rows[0:1, 0:512],
                                  rows[0:1, 0:512]))
    b.sV.sig(nc.vector.tensor_sub(
        rows[0:1, 1536:2048], rows[0:1, 512:1024], rows[0:1, 1024:1536]))
    v1 = b.sV.sig(nc.vector.tensor_scalar(
        rows[0:1, 512:1024], rows[0:1, 1536:2048], LN_EPS, None,
        op0=ALU.add))
    b.wait("scalar", v1)
    b.sA.sig(nc.scalar.activation(rows[0:1, 1024:1536], rows[0:1, 512:1024],
                                  AF.Ln))
    a3 = b.sA.sig(nc.scalar.activation(rows[0:1, 2048:2560],
                                       rows[0:1, 1024:1536], AF.Exp,
                                       scale=-0.5))
    b.wait("tensor", a3)
    b.sT.sig(nc.tensor.matmul(pst[:, :], g["ones_row"][:], rows[0:1, 0:512],
                              start=True, stop=True))
    bt2 = b.sT.sig(nc.tensor.matmul(pst2[:, :], g["ones_row"][:],
                                    rows[0:1, 2048:2560],
                                    start=True, stop=True))
    b.wait("scalar", bt2)
    c1 = b.sA.sig(nc.scalar.copy(mub[:], pst[:, :]))
    c2 = b.sA.sig(nc.scalar.copy(rsb[:], pst2[:, :]))
    pbank["pst"] = c1
    pbank["pst2"] = c2
    return c2


def ln1_phase(b, d, g, sq, s, x1sp):
    nc = b.nc
    b.barrier()
    attn_v = b.sV.tok()
    with ExitStack() as ph:
        Ep = ph.enter_context
        xt = Ep(nc.sbuf_tensor("xt", [128, 8, 512], F32))
        s_sb = Ep(nc.sbuf_tensor("s_sb", [128, 8, 512], F32))
        s2_sb = Ep(nc.sbuf_tensor("s2_sb", [128, 8, 512], F32))
        xrow = [Ep(nc.sbuf_tensor(f"lxrow{i}", [128, C], F32))
                for i in range(2)]
        sqt = [Ep(nc.sbuf_tensor(f"sqt{i}", [128, 512], F32))
               for i in range(2)]
        rows = Ep(nc.sbuf_tensor("rows", [1, 6 * 512], F32))
        mub = Ep(nc.sbuf_tensor("mub", [128, 512], F32))
        rsb = Ep(nc.sbuf_tensor("rsb", [128, 512], F32))
        ptx = [Ep(nc.psum_tensor(f"lptx{i}", [128, 512], F32))
               for i in range(2)]
        pst = Ep(nc.psum_tensor("pst", [128, 512], F32))
        pst2 = Ep(nc.psum_tensor("pst2", [128, 512], F32))

        pbank = {}
        spill_tok = None
        prev_cons = None
        pt_evac = [None, None]
        row_used = [None, None]
        for ch in range(NCH):
            for q in range(4):
                slr = q % 2
                b.wait("gpsimd", row_used[slr])
                ld = b.dma[f"xrow{slr}"].sig(nc.gpsimd.dma_start(
                    xrow[slr][:],
                    d["x"][s, ch * 512 + q * 128:ch * 512 + (q + 1) * 128, :]),
                    16)
                for dt in range(8):
                    pp = dt % 2
                    b.wait("tensor", ld, pt_evac[pp], prev_cons)
                    tk = b.sT.sig(nc.tensor.transpose(
                        ptx[pp][:, 0:128],
                        xrow[slr][:, dt * 128:(dt + 1) * 128], g["ident32"][:]))
                    dst = xt[:, dt, q * 128:(q + 1) * 128]
                    if dt % 2 == 0:
                        b.wait("vector", tk)
                        pt_evac[pp] = b.sV.sig(
                            nc.vector.tensor_copy(dst, ptx[pp][:, 0:128]))
                    else:
                        b.wait("scalar", tk)
                        pt_evac[pp] = b.sA.sig(
                            nc.scalar.copy(dst, ptx[pp][:, 0:128]))
                row_used[slr] = b.sT.tok()
            add_toks = []
            for c2t in range(8):
                r = c2t % 2
                b.wait("tensor", pt_evac[0], pt_evac[1], attn_v,
                       add_toks[c2t - 2] if c2t >= 2 else None)
                mm = None
                for ct in range(8):
                    mm = nc.tensor.matmul(
                        ptx[r][:], g["woutbf"][:, ct, c2t * 128:(c2t + 1) * 128],
                        sq["o_full"][:, ct, ch * 512:(ch + 1) * 512],
                        start=(ct == 0), stop=(ct == 7))
                mmk = b.sT.sig(mm)
                b.wait("vector", mmk, spill_tok)
                add_toks.append(b.sV.sig(nc.vector.tensor_add(
                    s_sb[:, c2t, :], ptx[r][:], xt[:, c2t, :])))
            stat_done = _ln_stats(b, g, s_sb, sqt, pst, pst2, add_toks, pbank)
            bc = _ln_rows_and_bcast(b, g, rows, pst, pst2, mub, rsb,
                                    stat_done, pbank)
            norm_toks = []
            for c2t in range(8):
                b.wait("vector", bc)
                b.sV.sig(nc.vector.tensor_sub(
                    s2_sb[:, c2t, :], s_sb[:, c2t, :], mub[:]))
                norm_toks.append(b.sV.sig(nc.vector.tensor_mul(
                    s_sb[:, c2t, :], s2_sb[:, c2t, :], rsb[:])))
            b.wait("gpsimd", *norm_toks)
            spill_tok = b.dma["spill"].sig(
                nc.gpsimd.dma_start(x1sp[s, ch], s_sb[:]), 16)
            prev_cons = b.sV.sig(nc.vector.tensor_copy(
                g["x1bf"][:, :, ch * 512:(ch + 1) * 512], s_sb[:]))
    b.debug_dump(f"x1bf{s}", g["x1bf"][:], [128, 8, T], BF16)


def ffn_phase(b, d, g, s, w1t, w2t, x1sp, y_out):
    nc = b.nc
    spills_done = b.dma["spill"].tok()
    b.pump(10 ** 9)
    b.barrier()
    b.wait("sync", b.dma["st0"].tok(), b.dma["st1"].tok())
    b.wait("tensor", b.dma["st0"].tok(), b.dma["st1"].tok())
    with ExitStack() as ph:
        Ep = ph.enter_context
        acc = Ep(nc.sbuf_tensor("acc", [128, 8, T], F32))
        pfn = Ep(nc.psum_tensor("pfn", [128, 8, 512], F32))

        bank_tok = [None] * 8
        acc_tok = {}
        _ffn_experts(b, d, g, s, w1t, w2t, acc, pfn, bank_tok, acc_tok)
        b.debug_dump(f"acc{s}", acc[:], [128, 8, T], F32)
        _ffn_epilogue(b, d, g, s, x1sp, y_out, acc, pfn, bank_tok, acc_tok,
                      spills_done)


def _ffn_experts(b, d, g, s, w1t, w2t, acc, pfn, bank_tok, acc_tok):
    nc = b.nc
    with ExitStack() as ph:
        Ep = ph.enter_context
        h_sb = Ep(nc.sbuf_tensor("h_sb", [128, 32, 512], BF16))
        w1r = [Ep(nc.sbuf_tensor(f"w1r{i}", [128, 8, 512], BF16))
               for i in range(2)]
        w2r = [Ep(nc.sbuf_tensor(f"w2r{i}", [128, 2, C], BF16))
               for i in range(2)]
        wb_sb = Ep(nc.sbuf_tensor("wb_sb", [128, 4, 512], F32))
        t1_sb = [Ep(nc.sbuf_tensor(f"t1sb{i}", [128, 512], F32))
                 for i in range(2)]
        wT_sb = Ep(nc.sbuf_tensor("wT_sb", [8, T], F32))

        router_tok = _router(b, d, g, s, wT_sb, pfn, bank_tok)

        t1_used = [None, None]
        for e in range(NE):
            sv_estart = b.sV.tok()
            wb_toks = []
            for ch in range(NCH):
                bk = ch % 8
                b.wait("tensor", bank_tok[bk], router_tok)
                tk = b.sT.sig(nc.tensor.matmul(
                    pfn[:, bk, :], g["sel"][0:8, e * 128:(e + 1) * 128],
                    wT_sb[0:8, ch * 512:(ch + 1) * 512],
                    start=True, stop=True))
                b.wait("scalar", tk, sv_estart)
                bank_tok[bk] = b.sA.sig(
                    nc.scalar.copy(wb_sb[:, ch, :], pfn[:, bk, :]))
                wb_toks.append(bank_tok[bk])
            w1_tok = [None, None]
            w1_used = [None, None]
            w2_tok = [None, None]
            w2_used = [None, None]
            for ch in range(NCH):
                for fb in range(8):
                    r = fb % 2
                    b.wait("sync", w1_used[r])
                    w1_tok[r] = b.dma[f"w1s{r}"].sig(nc.sync.dma_start(
                        w1r[r][:],
                        w1t[e][:, :, fb * 512:(fb + 1) * 512]
                        .rearrange("dt p f -> p dt f")), 16)
                    for ftb in range(4):
                        ft = fb * 4 + ftb
                        bk = ft % 8
                        b.wait("tensor", w1_tok[r], bank_tok[bk])
                        mm = None
                        for dt in range(8):
                            mm = nc.tensor.matmul(
                                pfn[:, bk, :],
                                w1r[r][:, dt, ftb * 128:(ftb + 1) * 128],
                                g["x1bf"][:, dt, ch * 512:(ch + 1) * 512],
                                start=(dt == 0), stop=(dt == 7))
                        mmk = b.sT.sig(mm)
                        b.wait("scalar", mmk)
                        bank_tok[bk] = b.sA.sig(nc.scalar.activation(
                            h_sb[:, ft, :], pfn[:, bk, :], AF.Relu,
                            bias=g["b1sb"][:, ft, e:e + 1], scale=1.0))
                    w1_used[r] = b.sT.tok()
                g1done = list(bank_tok)
                g2k = None
                for ftg in range(16):
                    r = ftg % 2
                    b.wait("sync", w2_used[r])
                    w2_tok[r] = b.dma[f"w2s{r}"].sig(nc.sync.dma_start(
                        w2r[r][:],
                        w2t[e][ftg * 2:(ftg + 1) * 2]
                        .rearrange("q p dd -> p q dd")), 16)
                    mm_last = None
                    first = True
                    for fq in range(2):
                        ft = ftg * 2 + fq
                        for ddt in range(8):
                            if ft == 0:
                                b.wait("tensor", w2_tok[r], g1done[ddt])
                            elif first:
                                b.wait("tensor", w2_tok[r])
                                first = False
                            mm_last = nc.tensor.matmul(
                                pfn[:, ddt, :],
                                w2r[r][:, fq, ddt * 128:(ddt + 1) * 128],
                                h_sb[:, ft, :],
                                start=(ft == 0), stop=(ft == 31))
                    g2k = b.sT.sig(mm_last)
                    w2_used[r] = g2k
                for ddt in range(8):
                    b.wait("vector", g2k, wb_toks[ch], t1_used[ddt % 2])
                    vt = b.sV.sig(nc.vector.scalar_tensor_tensor(
                        t1_sb[ddt % 2][:], pfn[:, ddt, :],
                        g["b2sb"][:, ddt, e:e + 1], wb_sb[:, ch, :],
                        op0=ALU.add, op1=ALU.mult))
                    bank_tok[ddt] = vt
                    b.wait("gpsimd", vt)
                    dst = acc[:, ddt, ch * 512:(ch + 1) * 512]
                    if e == 0:
                        t1_used[ddt % 2] = b.sP.sig(
                            nc.gpsimd.tensor_copy(dst, t1_sb[ddt % 2][:]))
                    else:
                        t1_used[ddt % 2] = b.sP.sig(
                            nc.gpsimd.tensor_add(dst, dst, t1_sb[ddt % 2][:]))
                    acc_tok[(ddt, ch)] = t1_used[ddt % 2]
            if e == 0:
                b.debug_dump(f"h_e0{s}", h_sb[:], [128, 32, 512], BF16)
                b.debug_dump(f"wb_e0{s}", wb_sb[:], [128, 4, 512], F32)
                b.debug_dump(f"acc_e0{s}", acc[:], [128, 8, T], F32)
        b.barrier()


def _router(b, d, g, s, wT_sb, pfn, bank_tok):
    nc = b.nc
    with ExitStack() as ph:
        Ep = ph.enter_context
        lg = Ep(nc.sbuf_tensor("lg", [128, 16, 8], F32))
        m1 = Ep(nc.sbuf_tensor("m1", [128, 16], F32))
        m2 = Ep(nc.sbuf_tensor("m2", [128, 16], F32))
        msk = Ep(nc.sbuf_tensor("msk", [128, 16, 8], F32))
        tmp = Ep(nc.sbuf_tensor("rtmp", [128, 16, 8], F32))
        den = Ep(nc.sbuf_tensor("den", [128, 16], F32))
        ev = []
        for tile in range(16):
            bk = tile % 8
            b.wait("tensor", bank_tok[bk])
            mm = None
            for dt in range(8):
                mm = nc.tensor.matmul(
                    pfn[:, bk, 0:8],
                    g["x1bf"][:, dt, tile * 128:(tile + 1) * 128],
                    g["wrtbf"][:, dt, :], start=(dt == 0), stop=(dt == 7))
            mmk = b.sT.sig(mm)
            b.wait("vector", mmk)
            bank_tok[bk] = b.sV.sig(nc.vector.tensor_tensor(
                lg[:, tile, :], pfn[:, bk, 0:8],
                g["brb"][:, None, :].broadcast_to((128, 1, 8))[:, 0, :],
                ALU.add))
            ev.append(bank_tok[bk])
        b.wait("vector", *ev)
        b.sV.sig(nc.vector.tensor_reduce(m1[:, :], lg[:, :, :], axis=AX.X,
                                         op=ALU.max))
        b.sV.sig(nc.vector.tensor_tensor(
            tmp[:, :, :], lg[:, :, :],
            m1[:, :, None].broadcast_to((128, 16, 8)), ALU.subtract))
        b.sV.sig(nc.vector.tensor_scalar(
            msk[:, :, :], tmp[:, :, :], 0.0, None, op0=ALU.is_equal))
        vtk = b.sV.tok()
        b.wait("scalar", vtk)
        e1tok = b.sA.sig(nc.scalar.activation(tmp[:, :, :], tmp[:, :, :],
                                              AF.Exp))
        b.wait("vector", e1tok)
        b.sV.sig(nc.vector.scalar_tensor_tensor(
            lg[:, :, :], msk[:, :, :], -BIG, lg[:, :, :],
            op0=ALU.mult, op1=ALU.add))
        b.sV.sig(nc.vector.tensor_reduce(m2[:, :], lg[:, :, :], axis=AX.X,
                                         op=ALU.max))
        b.sV.sig(nc.vector.tensor_tensor(
            lg[:, :, :], lg[:, :, :],
            m2[:, :, None].broadcast_to((128, 16, 8)), ALU.is_ge))
        b.sV.sig(nc.vector.tensor_add(msk[:, :, :], msk[:, :, :],
                                      lg[:, :, :]))
        v2 = b.sV.sig(nc.vector.tensor_sub(m2[:, :], m2[:, :], m1[:, :]))
        b.wait("scalar", v2)
        a2 = b.sA.sig(nc.scalar.activation(den[:, :], m2[:, :], AF.Exp))
        b.wait("vector", a2)
        b.sV.sig(nc.vector.tensor_scalar(den[:, :], den[:, :], 1.0, None,
                                         op0=ALU.add))
        b.sV.sig(nc.vector.reciprocal(den[:, :], den[:, :]))
        b.sV.sig(nc.vector.tensor_mul(tmp[:, :, :], tmp[:, :, :],
                                      msk[:, :, :]))
        wtok = b.sV.sig(nc.vector.tensor_tensor(
            tmp[:, :, :], tmp[:, :, :],
            den[:, :, None].broadcast_to((128, 16, 8)), ALU.mult))
        b.debug_dump(f"rw{s}", tmp[:], [128, 16, 8], F32)
        last = None
        for tile in range(16):
            bk = tile % 8
            b.wait("tensor", wtok, bank_tok[bk])
            tk = b.sT.sig(nc.tensor.transpose(
                pfn[0:8, bk, 0:128], tmp[:, tile, :], g["ident32"][:]))
            b.wait("vector", tk)
            bank_tok[bk] = b.sV.sig(nc.vector.tensor_copy(
                wT_sb[0:8, tile * 128:(tile + 1) * 128], pfn[0:8, bk, 0:128]))
            last = bank_tok[bk]
        b.debug_dump(f"rwT{s}", wT_sb[:], [8, T], F32)
        return last


def _ffn_epilogue(b, d, g, s, x1sp, y_out, acc, pfn, bank_tok, acc_tok,
                  spills_done):
    nc = b.nc
    with ExitStack() as ph:
        Ep = ph.enter_context
        x1re = Ep(nc.sbuf_tensor("x1re", [128, 8, 512], F32))
        s_sb = Ep(nc.sbuf_tensor("es_sb", [128, 8, 512], F32))
        s2_sb = Ep(nc.sbuf_tensor("es2_sb", [128, 8, 512], F32))
        sqt = [Ep(nc.sbuf_tensor(f"esqt{i}", [128, 512], F32))
               for i in range(2)]
        rows = Ep(nc.sbuf_tensor("erows", [1, 5 * 512], F32))
        mub = Ep(nc.sbuf_tensor("emub", [128, 512], F32))
        rsb = Ep(nc.sbuf_tensor("ersb", [128, 512], F32))
        ostage = Ep(nc.sbuf_tensor("ostage", [128, 4, C], F32))

        pbank = {"pst": bank_tok[0], "pst2": bank_tok[1]}
        out_tok = None
        prev_used = None
        for ch in range(NCH):
            b.wait("gpsimd", prev_used, spills_done)
            reload_tok = b.dma["reload"].sig(
                nc.gpsimd.dma_start(x1re[:], x1sp[s, ch]), 16)
            add_toks = []
            for c2t in range(8):
                b.wait("vector", reload_tok, acc_tok[(c2t, ch)])
                add_toks.append(b.sV.sig(nc.vector.tensor_add(
                    s_sb[:, c2t, :], x1re[:, c2t, :],
                    acc[:, c2t, ch * 512:(ch + 1) * 512])))
            stat_done = _ln_stats(b, g, s_sb, sqt, pfn[:, 0, :], pfn[:, 1, :],
                                  add_toks, pbank)
            bc = _ln_rows_and_bcast(b, g, rows, pfn[:, 0, :], pfn[:, 1, :],
                                    mub, rsb, stat_done, pbank)
            x2_toks = []
            for c2t in range(8):
                b.wait("vector", bc)
                b.sV.sig(nc.vector.tensor_sub(
                    s2_sb[:, c2t, :], s_sb[:, c2t, :], mub[:]))
                b.sV.sig(nc.vector.tensor_mul(
                    s_sb[:, c2t, :], s2_sb[:, c2t, :], rsb[:]))
                x2_toks.append(b.sV.sig(nc.vector.tensor_scalar(
                    s2_sb[:, c2t, :], s_sb[:, c2t, :],
                    g["gsb"][:, c2t:c2t + 1], g["bsb"][:, c2t:c2t + 1],
                    op0=ALU.mult, op1=ALU.add)))
            add2 = []
            for c2t in range(8):
                b.wait("vector", x2_toks[c2t])
                add2.append(b.sV.sig(nc.vector.tensor_add(
                    s_sb[:, c2t, :], x1re[:, c2t, :], s2_sb[:, c2t, :])))
            stat2 = _ln_stats(b, g, s_sb, sqt, pfn[:, 0, :], pfn[:, 1, :],
                              add2, pbank)
            bc2 = _ln_rows_and_bcast(b, g, rows, pfn[:, 0, :], pfn[:, 1, :],
                                     mub, rsb, stat2, pbank)
            onrm = []
            for c2t in range(8):
                b.wait("vector", bc2)
                b.sV.sig(nc.vector.tensor_sub(
                    s2_sb[:, c2t, :], s_sb[:, c2t, :], mub[:]))
                onrm.append(b.sV.sig(nc.vector.tensor_mul(
                    s_sb[:, c2t, :], s2_sb[:, c2t, :], rsb[:])))
            tr_evac = [None, None]
            for c2t in range(8):
                for q in range(4):
                    idx = (c2t * 4 + q) % 2
                    bk = 2 + idx
                    b.wait("tensor", onrm[c2t], tr_evac[idx], out_tok,
                           bank_tok[bk])
                    tk = b.sT.sig(nc.tensor.transpose(
                        pfn[:, bk, 0:128],
                        s_sb[:, c2t, q * 128:(q + 1) * 128], g["ident32"][:]))
                    dst = ostage[:, q, c2t * 128:(c2t + 1) * 128]
                    if idx == 0:
                        b.wait("vector", tk, out_tok)
                        tr_evac[idx] = b.sV.sig(
                            nc.vector.tensor_copy(dst, pfn[:, bk, 0:128]))
                    else:
                        b.wait("scalar", tk, out_tok)
                        tr_evac[idx] = b.sA.sig(
                            nc.scalar.copy(dst, pfn[:, bk, 0:128]))
                    bank_tok[bk] = tr_evac[idx]
            b.wait("gpsimd", tr_evac[0], tr_evac[1])
            out_tok = b.dma["out"].sig(nc.gpsimd.dma_start(
                y_out[s, ch * 512:(ch + 1) * 512, :]
                .rearrange("(q p) c -> p q c", p=128), ostage[:]), 16)
            prev_used = b.sV.tok()
        b.wait("gpsimd", out_tok)


def kernel(**inputs):
    consts = host_constants()
    x = np.ascontiguousarray(np.asarray(inputs["L0_attn_x"], dtype=np.float32))
    shards = [x[i * SEQ_PER_CORE:(i + 1) * SEQ_PER_CORE] for i in range(NCORE)]
    base = {
        "w_qkv": np.ascontiguousarray(inputs["L0_attn_w_qkv"],
                                      dtype=np.float32),
        "w_out": np.ascontiguousarray(inputs["L0_attn_w_out"],
                                      dtype=np.float32),
        "Wr": np.ascontiguousarray(inputs["L0_ffn_Wr"], dtype=np.float32),
        "br": np.ascontiguousarray(inputs["L0_ffn_br"],
                                   dtype=np.float32).reshape(1, NE),
        "W1": np.ascontiguousarray(inputs["L0_ffn_W1"], dtype=np.float32),
        "b1": np.ascontiguousarray(inputs["L0_ffn_b1"], dtype=np.float32),
        "W2": np.ascontiguousarray(inputs["L0_ffn_W2"], dtype=np.float32),
        "b2": np.ascontiguousarray(inputs["L0_ffn_b2"], dtype=np.float32),
        "gamma": np.ascontiguousarray(inputs["L0_ffn_gamma"],
                                      dtype=np.float32).reshape(8, 128),
        "beta": np.ascontiguousarray(inputs["L0_ffn_beta"],
                                     dtype=np.float32).reshape(8, 128),
    }
    base.update(consts)
    nc, _ = build_program()
    in_maps = [dict(base, x=sh) for sh in shards]
    res = run_bass_kernel_spmd(nc, in_maps, core_ids=list(range(NCORE)))
    return np.concatenate([r["y"] for r in res.results], axis=0)


if __name__ == "__main__":
    nc, b = build_program(n_seq=1)
    print("built ok; instructions:", nc.n_instructions())
